# version 02 (final)
# baseline (speedup 1.0000x reference)
import numpy as np
import ml_dtypes
import concourse.bass as bass
import concourse.mybir as mybir
from concourse.bass_utils import run_bass_kernel_spmd

F32 = mybir.dt.float32
BF16 = mybir.dt.bfloat16
I8 = mybir.dt.int8
I32 = mybir.dt.int32
AF = mybir.ActivationFunctionType
ALU = mybir.AluOpType
AX = mybir.AxisListType

BF = ml_dtypes.bfloat16

D = 4096
NF = 32
IFF = 11008
NI = 86
TOK = 1024
TS = 512
H = 32
KVH = 8
DH = 128
CD = 2048
SCALE = float(DH) ** -0.5
M0 = 48.0
KC_SLOT = (32, 16)

STOP_AFTER = None
DEBUG_LN_NOREUSE = False
DBG_BR = (0, 1, 2)



def _bf(x):
    return np.ascontiguousarray(np.asarray(x, dtype=np.float32).astype(BF))


def stage_shared(inp):
    st = {}
    gq = np.asarray(inp["L0_ffn_gate_q"])
    uq = np.asarray(inp["L0_ffn_up_q"])
    dq = np.asarray(inp["L0_ffn_down_q"])
    g4 = gq.reshape(NI, 128, NF, 128).transpose(0, 3, 2, 1).reshape(NI, 128, D)
    u4 = uq.reshape(NI, 128, NF, 128).transpose(0, 3, 2, 1).reshape(NI, 128, D)
    st["wguq8"] = np.ascontiguousarray(
        np.concatenate([g4, u4], axis=2).astype(np.int8))
    st["wdq8"] = np.ascontiguousarray(
        dq.reshape(NF, 128, NI, 128).transpose(0, 3, 2, 1).reshape(NF, 128, IFF).astype(np.int8))
    st["gmatB"] = np.ascontiguousarray(np.asarray(inp["L0_ffn_gate_B"], np.float32))
    st["umatB"] = np.ascontiguousarray(np.asarray(inp["L0_ffn_up_B"], np.float32))
    st["dmatB"] = np.ascontiguousarray(np.asarray(inp["L0_ffn_down_B"], np.float32))
    ga = np.asarray(inp["L0_ffn_gate_A"]).reshape(NF, 128, 32)
    ua = np.asarray(inp["L0_ffn_up_A"]).reshape(NF, 128, 32)
    st["aaug"] = np.ascontiguousarray(np.concatenate(
        [ga, ua, np.ones((NF, 128, 1), np.float32)], axis=2).astype(np.float32))
    da = np.asarray(inp["L0_ffn_down_A"]).reshape(NI, 128, 32)
    st["daug"] = np.ascontiguousarray(np.concatenate(
        [da, np.ones((NI, 128, 1), np.float32)], axis=2).astype(np.float32))
    st["scg"] = np.ascontiguousarray(np.asarray(inp["L0_ffn_gate_scale"], np.float32).reshape(NI, 128).T)
    st["gsc_row"] = np.ascontiguousarray(np.asarray(inp["L0_ffn_gate_scale"], np.float32).reshape(1, IFF))
    st["usc_row"] = np.ascontiguousarray(np.asarray(inp["L0_ffn_up_scale"], np.float32).reshape(1, IFF))
    st["dsc_row"] = np.ascontiguousarray(np.asarray(inp["L0_ffn_down_scale"], np.float32).reshape(1, D))
    st["scu"] = np.ascontiguousarray(np.asarray(inp["L0_ffn_up_scale"], np.float32).reshape(NI, 128).T)
    st["scd"] = np.ascontiguousarray(np.asarray(inp["L0_ffn_down_scale"], np.float32).reshape(NF, 128).T)

    wqkv = np.asarray(inp["L0_attn_w_qkv"])
    st["wqkvT"] = np.ascontiguousarray(
        wqkv.reshape(48, 128, NF, 128).transpose(0, 3, 2, 1).reshape(48, 128, D))
    wo = np.asarray(inp["L0_attn_w_o"])
    st["woT"] = np.ascontiguousarray(
        wo.reshape(NF, 128, NF, 128).transpose(0, 3, 2, 1).reshape(NF, 128, D))
    for nm, w in (("wk1T", "L0_attn_k_w1"), ("wv1T", "L0_attn_v_w1")):
        ww = np.asarray(inp[w])
        st[nm] = np.ascontiguousarray(
            ww.reshape(16, 128, 16, 128).transpose(0, 3, 2, 1).reshape(16, 128, CD))
    for nm, w in (("wk2T", "L0_attn_k_w2"), ("wv2T", "L0_attn_v_w2")):
        ww = np.asarray(inp[w])
        st[nm] = np.ascontiguousarray(ww.T.reshape(16, 128, 128).astype(np.float32))
    st["wgT"] = np.ascontiguousarray(
        np.asarray(inp["L0_attn_w_gate"]).T.reshape(NF, 128, 96).astype(np.float32))
    st["kb1"] = np.ascontiguousarray(np.asarray(inp["L0_attn_k_b1"], np.float32).reshape(16, 128).T)
    st["vb1"] = np.ascontiguousarray(np.asarray(inp["L0_attn_v_b1"], np.float32).reshape(16, 128).T)
    st["kb2"] = np.ascontiguousarray(np.asarray(inp["L0_attn_k_b2"], np.float32).reshape(128, 1))
    st["vb2"] = np.ascontiguousarray(np.asarray(inp["L0_attn_v_b2"], np.float32).reshape(128, 1))
    st["bg"] = np.ascontiguousarray(np.asarray(inp["L0_attn_b_gate"], np.float32).reshape(96, 1))
    kp = np.asarray(inp["L0_attn_k_pos"])
    vp = np.asarray(inp["L0_attn_v_pos"])
    st["kpos"] = _bf(np.tile(kp.transpose(0, 2, 1), (1, 1, TOK // 16)))
    st["vpos"] = _bf(np.tile(vp.transpose(0, 2, 1), (1, 1, TOK // 16)))
    mem = np.asarray(inp["L0_attn_mem_kv"])
    st["memk"] = _bf(mem[0, :, 0, :].reshape(KVH, 128, 1))
    st["memv"] = _bf(mem[1, :, 0, :].reshape(KVH, 1, 128))

    st["ident"] = _bf(np.eye(128, dtype=np.float32))
    sw = np.zeros((128, 128), np.float32)
    for i in range(64):
        sw[2 * i, 2 * i + 1] = 1.0
        sw[2 * i + 1, 2 * i] = 1.0
    st["swapm"] = _bf(sw)
    ee = np.zeros((16, 128, 128), np.float32)
    for v in range(16):
        for k in range(128):
            ee[v, v * 8 + k // 16, k] = 1.0
    st["eexp"] = _bf(ee)
    p = np.arange(128)
    dg = ((p[:, None] // 16 == p[None, :] // 16) & (p[:, None] <= p[None, :]))
    st["diagm"] = _bf(dg.astype(np.float32))
    st["onescol_b"] = _bf(np.ones((128, 1), np.float32))
    st["onesrow_b"] = _bf(np.ones((1, 128), np.float32))
    st["onescol_f"] = np.ones((1, 128), np.float32)
    return st


def core_chunks(r):
    b, rr = r // 4, r % 4
    return b, 7 - rr, rr


def stage_percore(inp, r):
    x = np.asarray(inp["L0_attn_hidden_states"])
    b, cA, cB = core_chunks(r)
    t0A, t0B = cA * 512, cB * 512
    pc = {}
    pc["xin"] = np.ascontiguousarray(
        np.concatenate([x[b, t0A:t0A + 512, :].T, x[b, t0B:t0B + 512, :].T], axis=1))
    pos = np.concatenate([np.arange(t0A, t0A + 512), np.arange(t0B, t0B + 512)])
    inv = 1.0 / 10000.0 ** (np.arange(0, DH, 2, dtype=np.float64) / DH)
    ang = pos[None, :] * inv[:, None]
    cos = np.repeat(np.cos(ang), 2, axis=0).astype(np.float32)
    sin = np.repeat(np.sin(ang), 2, axis=0).astype(np.float32)
    sgn = np.where(np.arange(128) % 2 == 0, -1.0, 1.0)[:, None].astype(np.float32)
    pc["rcos"] = _bf(cos)
    pc["rsin"] = _bf(sin * sgn)
    slide = np.zeros((2, 5, 128, TS), np.float32)
    maskc = np.zeros((2, 3, 128, TS), np.float32)
    maskci = np.zeros((2, 4, 128, 384), np.float32)
    for s, t0 in ((0, t0A), (1, t0B)):
        t = t0 + np.arange(TS)[None, :]
        for kc in range(5):
            kg = t0 - 128 + kc * 128 + np.arange(128)[:, None]
            slide[s, kc] = ((kg >= 0) & (t - kg >= 0) & (t - kg <= 64)).astype(np.float32)
        for kc in range(3):
            jj = kc * 128 + np.arange(128)[:, None]
            valid = (jj == 0) | ((jj <= 256) & (16 * jj <= t))
            maskc[s, kc] = valid.astype(np.float32)
        for jt in range(4):
            tt = t0 + jt * 128 + np.arange(128)[:, None]
            jj = np.arange(384)[None, :]
            maskci[s, jt] = ((jj >= 1) & (jj <= 256) & (16 * jj <= tt)).astype(np.float32)
    pc["slidem"] = _bf(slide)
    pc["maskc"] = _bf(maskc)
    pc["maskci"] = _bf(maskci)

    def flat(g):
        if g < 0:
            return 0
        return (g * 2 + 1) if g < 4 else ((7 - g) * 2 + 0)
    pc["preds"] = np.array([[flat(cA - 1), flat(cB - 1)]], dtype=np.int32)
    return pc



NLANES = 8


class Em:
    def __init__(self, nc, ctx):
        self.nc = nc
        self.t = nc.tensor
        self.v = nc.vector
        self.a = nc.scalar
        self.g = nc.gpsimd
        self.y = nc.sync
        self.sh = {}
        self.sc = {}
        self.wc = {}
        self.bfree = [None] * 8
        self.dln = {"DY": 0, "DG": 0}
        sems = ["T", "V", "A", "G", "CC"]
        sems += ["DY%d" % i for i in range(NLANES)] + ["DG%d" % i for i in range(NLANES)]
        for s in sems:
            self.sh[s] = ctx.enter_context(nc.semaphore(s))
            self.sc[s] = 0

    def ev(self, instr, sem, amt=1):
        self.sc[sem] += amt
        instr.then_inc(self.sh[sem], amt)
        return (sem, self.sc[sem])

    def wait(self, eng, *evs):
        flat = []
        for e in evs:
            if e is None:
                continue
            if isinstance(e, (list, tuple)) and e and isinstance(e[0], (list, tuple)):
                flat.extend(e)
            else:
                flat.append(e)
        for e in flat:
            if e is None:
                continue
            s, v = e
            k = (id(eng), s)
            if self.wc.get(k, 0) >= v:
                continue
            self.wc[k] = v
            eng.wait_ge(self.sh[s], v)

    def dfence(self, ev):
        return self.dma(self.g, "DG", self.fence_dram, self.fence_tile[0:1, 0:16],
                        waits=(ev,))

    def fence(self, q):
        return tuple((q + str(i), self.sc[q + str(i)]) for i in range(NLANES)
                     if self.sc[q + str(i)])

    def dma(self, eng, q, out, in_, waits=(), **kw):
        lane = q + str(self.dln[q] % NLANES)
        self.dln[q] += 1
        self.wait(eng, (lane, self.sc[lane]))
        self.wait(eng, *waits)
        try:
            i = eng.dma_start(out=out, in_=in_, **kw)
        except Exception:
            import traceback
            print("DMA FAIL out=", out, "in=", in_)
            traceback.print_exc()
            raise
        e = self.ev(i, lane, 16)
        self.last_dma_ev = e
        return e



class Pool:
    def __init__(self, tile, nelem):
        self.tile = tile
        self.n = nelem
        self.off = 0
        self.marks = []
        self.peak = 0

    def al(self, rows, cols):
        o = self.off
        assert o + cols <= self.n, f"pool overflow {o}+{cols}>{self.n}"
        self.off += cols
        self.peak = max(self.peak, self.off)
        return self.tile[0:rows, o:o + cols]

    def push(self):
        self.marks.append(self.off)

    def pop(self):
        self.off = self.marks.pop()


BPOOL_N = 86500
FPOOL_N = 7400


def build_program():
    import contextlib
    nc = bass.Bass("TRN2", target_bir_lowering=False, debug=False)
    ctx = contextlib.ExitStack()

    dr = {}
    dr["xin"] = nc.dram_tensor("xin", [D, TOK], F32, kind="ExternalInput").ap()
    dr["yout"] = nc.dram_tensor("yout", [D, TOK], F32, kind="ExternalOutput").ap()
    shared_specs = {
        "wguq8": ([NI, 128, 2 * D], I8), "wdq8": ([NF, 128, IFF], I8),
        "gmatB": ([32, IFF], F32), "umatB": ([32, IFF], F32), "dmatB": ([32, D], F32),
        "aaug": ([NF, 128, 65], F32), "daug": ([NI, 128, 33], F32),
        "scg": ([128, NI], F32), "scu": ([128, NI], F32), "scd": ([128, NF], F32),
        "gsc_row": ([1, IFF], F32), "usc_row": ([1, IFF], F32), "dsc_row": ([1, D], F32),
        "wqkvT": ([48, 128, D], F32), "woT": ([NF, 128, D], F32),
        "wk1T": ([16, 128, CD], F32), "wv1T": ([16, 128, CD], F32),
        "wk2T": ([16, 128, 128], F32), "wv2T": ([16, 128, 128], F32),
        "wgT": ([NF, 128, 96], F32),
        "kb1": ([128, 16], F32), "vb1": ([128, 16], F32),
        "kb2": ([128, 1], F32), "vb2": ([128, 1], F32), "bg": ([96, 1], F32),
        "kpos": ([KVH, 128, TOK], BF16), "vpos": ([KVH, 128, TOK], BF16),
        "memk": ([KVH, 128, 1], BF16), "memv": ([KVH, 1, 128], BF16),
        "ident": ([128, 128], BF16), "swapm": ([128, 128], BF16),
        "eexp": ([16, 128, 128], BF16), "diagm": ([128, 128], BF16),
        "onescol_b": ([128, 1], BF16), "onesrow_b": ([1, 128], BF16),
        "onescol_f": ([1, 128], F32),
        "rcos": ([128, TOK], BF16), "rsin": ([128, TOK], BF16),
        "slidem": ([2, 5, 128, TS], BF16), "maskc": ([2, 3, 128, TS], BF16),
        "maskci": ([2, 4, 128, 384], BF16), "preds": ([1, 2], I32),
    }
    for nm, (shp, dt) in shared_specs.items():
        dr[nm] = nc.dram_tensor(nm, shp, dt, kind="ExternalInput").ap()

    dr["xlnsp"] = nc.dram_tensor("xlnsp", [D, TOK], BF16).ap()
    dr["xres1"] = nc.dram_tensor("xres1", [D, TOK], F32).ap()
    dr["xres2"] = nc.dram_tensor("xres2", [D, TOK], F32).ap()
    dr["ctr_rk"] = nc.dram_tensor("ctr_rk", [2, KVH * 128, TS], BF16).ap()
    dr["ctr_v"] = nc.dram_tensor("ctr_v", [2, KVH, TS, 128], BF16).ap()
    dr["ctr_ck"] = nc.dram_tensor("ctr_ck", [2, KVH, 128, 32], BF16).ap()
    dr["ctr_cv"] = nc.dram_tensor("ctr_cv", [2, KVH, 32, 128], BF16).ap()
    dr["gat_rk"] = nc.dram_tensor("gat_rk", [8, KVH * 128, TS], BF16).ap()
    dr["gat_v"] = nc.dram_tensor("gat_v", [8, KVH, TS, 128], BF16).ap()
    dr["gat_ck"] = nc.dram_tensor("gat_ck", [8, KVH, 128, 32], BF16).ap()
    dr["gat_cv"] = nc.dram_tensor("gat_cv", [8, KVH, 32, 128], BF16).ap()
    dr["slp_rk"] = nc.dram_tensor("slp_rk", [2, KVH * 128, 128], BF16).ap()
    dr["fdump"] = nc.dram_tensor("fdump", [1, 16], BF16).ap()
    dr["slp_v"] = nc.dram_tensor("slp_v", [2, KVH, 128, 128], BF16).ap()

    BPOOL = ctx.enter_context(nc.sbuf_tensor("bpool", [128, BPOOL_N], BF16))
    FPOOL = ctx.enter_context(nc.sbuf_tensor("fpool", [128, FPOOL_N], F32))
    pb = Pool(BPOOL, BPOOL_N)
    pf = Pool(FPOOL, FPOOL_N)
    banks = [ctx.enter_context(nc.psum_tensor("bank%d" % i, [128, 512], F32))
             for i in range(8)]

    em = Em(nc, ctx)
    em.fence_dram = dr["fdump"]
    with nc.Block() as block:
        @block.sync
        def _(sy):
            emit_all(em, dr, pb, pf, banks)
    ctx.close()
    return nc



def emit_all(em, dr, pb, pf, banks):
    C = {}

    def ld(pool, name, rows, cols, src, eng=None, sem=None):
        tile = pool.al(rows, cols)
        e = em.dma(eng or em.y, sem or "DY", tile[...], src)
        C[name] = (tile, e)
        return tile

    def ld3(pool, name, rows, per, src3, eng=None, sem=None):
        n = src3.shape[0]
        tile = pool.al(rows, n * per)
        e = None
        for i in range(n):
            e = em.dma(eng or em.y, sem or "DY", tile[0:rows, i * per:(i + 1) * per],
                       src3[i])
        C[name] = (tile, e)
        return tile

    ld(pb, "ident", 128, 128, dr["ident"])
    ld(pb, "swapm", 128, 128, dr["swapm"])
    ld(pb, "diagm", 128, 128, dr["diagm"])
    ld(pb, "onescol_b", 128, 1, dr["onescol_b"])
    ld(pb, "onesrow_b", 1, 128, dr["onesrow_b"])
    ld(pf, "onescol_f", 1, 128, dr["onescol_f"])
    ld(pf, "kb1", 128, 16, dr["kb1"])
    ld(pf, "vb1", 128, 16, dr["vb1"])
    ld(pf, "kb2", 128, 1, dr["kb2"])
    ld(pf, "vb2", 128, 1, dr["vb2"])
    ld(pf, "bg", 96, 1, dr["bg"])
    ld(pf, "scg_r", 128, NI, dr["scg"])
    ld(pf, "scu_r", 128, NI, dr["scu"])
    ld(pf, "scd_r", 128, NF, dr["scd"])
    ld3(pb, "aaug", 128, 65, dr["aaug"], em.g, "DG")
    ld3(pb, "daug", 128, 33, dr["daug"], em.g, "DG")
    ld3(pb, "memk", 128, 1, dr["memk"])

    em.fence_tile = pb.al(1, 16)
    m48 = pf.al(128, 1)
    epsc = pf.al(1, 1)
    C["m48"] = (m48, em.ev(em.v.memset(m48[...], -M0), "V"))
    C["epsc"] = (epsc, em.ev(em.v.memset(epsc[...], 1e-5), "V"))
    scg = pf.al(128, NI)
    scu = pf.al(128, NI)
    scd05 = pf.al(128, NF)
    em.wait(em.v, C["scg_r"][1])
    C["scg"] = (scg, em.ev(em.v.tensor_scalar(
        scg[...], C["scg_r"][0][...], 0.005, 1e-25, ALU.mult, ALU.max), "V"))
    em.wait(em.v, C["scu_r"][1])
    C["scu"] = (scu, em.ev(em.v.tensor_scalar(
        scu[...], C["scu_r"][0][...], 0.005, 1e-25, ALU.mult, ALU.max), "V"))
    em.wait(em.v, C["scd_r"][1])
    C["scd05"] = (scd05, em.ev(em.v.tensor_scalar(
        scd05[...], C["scd_r"][0][...], 0.0025, 5e-26, ALU.mult, ALU.max), "V"))

    st = dict(C=C, pb=pb, pf=pf, banks=banks, dr=dr, ld=ld, ld3=ld3)

    if STOP_AFTER == "COPY":
        pf.push()
        t = [pf.al(128, TS) for _ in range(2)]
        prev = [None, None]
        for c in range(NF):
            for s in range(2):
                j = c * 2 + s
                e = em.dma(em.y, "DY", t[j % 2][...],
                           dr["xin"][c * 128:(c + 1) * 128, s * TS:(s + 1) * TS],
                           waits=((prev[j % 2],) if prev[j % 2] else ()))
                prev[j % 2] = em.dma(em.y, "DY",
                                     dr["yout"][c * 128:(c + 1) * 128, s * TS:(s + 1) * TS],
                                     t[j % 2][...], waits=(e,))
        pf.pop()
        finish(em)
        return

    if STOP_AFTER == "LN":
        pb.push()
        xln = [pb.al(128, TS) for _ in range(NF)]
        scr = [pb.al(128, TS) for _ in range(4)]
        lastd = None
        for s in range(2):
            if lastd is not None:
                em.wait(em.v, lastd)
            ev_xln = layer_norm(em, st, dr["xin"], s, xln, scr)
            for c in range(NF):
                lastd = em.dma(em.g, "DG", dr["yout"][c * 128:(c + 1) * 128, s * TS:(s + 1) * TS],
                               xln[c][...], waits=(ev_xln[c],))
        pb.pop()
        finish(em)
        return

    e1 = ffn_sublayer(em, st, src=dr["xin"],
                      dst=(dr["yout"] if STOP_AFTER == "P1" else dr["xres1"]))
    if STOP_AFTER == "P1":
        finish(em)
        return
    if STOP_AFTER == "P1P3":
        e2 = attention_stub(em, st, e1)
        ffn_sublayer(em, st, src=dr["xres2"], dst=dr["yout"], src_ready=e2)
    elif STOP_AFTER in ("ATT", "QKV", "XLN2", "XLN2A", "RQD", "SLKD"):
        attention(em, st, e1)
    elif STOP_AFTER == "P2":
        e2 = attention(em, st, e1)
        pf.push()
        t = pf.al(128, TS)
        prev = e2
        for c in range(NF):
            for s in range(2):
                e = em.dma(em.g, "DG", t[...],
                           dr["xres2"][c * 128:(c + 1) * 128, s * TS:(s + 1) * TS],
                           waits=((prev,) if prev else ()))
                prev = em.dma(em.g, "DG",
                              dr["yout"][c * 128:(c + 1) * 128, s * TS:(s + 1) * TS],
                              t[...], waits=(e,))
        pf.pop()
    else:
        e2 = attention(em, st, e1)
        ffn_sublayer(em, st, src=dr["xres2"], dst=dr["yout"], src_ready=e2)
    finish(em)


def finish(em):
    for q in ("DY", "DG"):
        for i in range(NLANES):
            lane = q + str(i)
            if em.sc[lane]:
                em.y.wait_ge(em.sh[lane], em.sc[lane])


def attention_stub(em, st, src_ready=None):
    dr, pf = st["dr"], st["pf"]
    pf.push()
    t = pf.al(128, TS)
    prev = src_ready
    for c in range(NF):
        for s in range(2):
            e = em.dma(em.g, "DG", t[...],
                       dr["xres1"][c * 128:(c + 1) * 128, s * TS:(s + 1) * TS],
                       waits=((prev,) if prev else ()))
            prev = em.dma(em.g, "DG",
                          dr["xres2"][c * 128:(c + 1) * 128, s * TS:(s + 1) * TS],
                          t[...], waits=(e,))
    pf.pop()
    return em.fence("DG") + em.fence("DY")


def layer_norm(em, st, src, s, xln_tiles, bscratch, src_ready=None):
    C, pb, pf, banks = st["C"], st["pb"], st["pf"], st["banks"]
    pb.push(); pf.push()
    xrot = [pf.al(128, TS) for _ in range(3)]
    xbrot = bscratch[0:2]
    x2rot = bscratch[2:4]
    SX = banks[6].ap()[0:1, :]
    SXX = banks[7].ap()[0:1, :]
    ev_dve = [None] * NF
    ev_mm = [None] * NF
    for c in range(NF):
        e_x = em.dma(em.g, "DG", xrot[c % 3][...],
                     src[c * 128:(c + 1) * 128, s * TS:(s + 1) * TS],
                     waits=([ev_dve[c - 3]] if c >= 3 else [src_ready]))
        w = [e_x, C["onescol_b"][1]]
        if c >= 2:
            w.append(ev_mm[c - 2])
        em.wait(em.v, *w)
        i1 = em.v.tensor_copy(xbrot[c % 2][...], xrot[c % 3][...])
        em.ev(i1, "V")
        i2 = em.v.tensor_tensor(x2rot[c % 2][...], xrot[c % 3][...], xrot[c % 3][...], ALU.mult)
        ev_dve[c] = em.ev(i2, "V")
        w = [ev_dve[c]]
        if c == 0:
            w += [em.bfree[6], em.bfree[7]]
        em.wait(em.t, *w)
        em.t.matmul(SX, C["onescol_b"][0][...], xbrot[c % 2][...],
                    start=(c == 0), stop=(c == NF - 1))
        i = em.t.matmul(SXX, C["onescol_b"][0][...], x2rot[c % 2][...],
                        start=(c == 0), stop=(c == NF - 1))
        ev_mm[c] = em.ev(i, "T")
    m = pf.al(1, TS)
    e2r = pf.al(1, TS)
    var = pf.al(1, TS)
    sd = pf.al(1, TS)
    arow = pf.al(1, TS)
    brow = pf.al(1, TS)
    em.wait(em.v, ev_mm[NF - 1])
    em.ev(em.v.tensor_scalar(m[...], SX, 1.0 / D, None, ALU.mult), "V")
    ev = em.ev(em.v.tensor_scalar(e2r[...], SXX, 1.0 / D, None, ALU.mult), "V")
    em.wait(em.v, ev)
    ev = em.ev(em.v.tensor_tensor(var[...], m[...], m[...], ALU.mult), "V")
    em.wait(em.v, ev)
    ev = em.ev(em.v.tensor_tensor(var[...], e2r[...], var[...], ALU.subtract), "V")
    em.wait(em.a, ev, C["epsc"][1])
    ev = em.ev(em.a.activation(sd[...], var[...], AF.Sqrt, bias=C["epsc"][0][...], scale=1.0), "A")
    em.wait(em.v, ev)
    ev = em.ev(em.v.reciprocal(arow[...], sd[...]), "V")
    em.wait(em.v, ev)
    ev = em.ev(em.v.tensor_tensor(brow[...], m[...], arow[...], ALU.mult), "V")
    em.wait(em.v, ev)
    evb = em.ev(em.v.tensor_scalar(brow[...], brow[...], -1.0, None, ALU.mult), "V")
    em.wait(em.t, evb, C["onescol_f"][1])
    em.t.matmul(banks[6][...], C["onescol_f"][0][...], arow[...], start=True, stop=True)
    i = em.t.matmul(banks[7][...], C["onescol_f"][0][...], brow[...], start=True, stop=True)
    ebc = em.ev(i, "T")
    tmp = pf.al(128, TS)
    ev_norm = [None] * NF
    ev_t1 = [None] * NF
    prev_tt = None
    if DEBUG_LN_NOREUSE:
        pb.push()
        xall = [pb.al(128, TS) for _ in range(NF)]
        evs = []
        for c in range(NF):
            evs.append(em.dma(em.g, "DG", xall[c][...],
                              src[c * 128:(c + 1) * 128, s * TS:(s + 1) * TS],
                              waits=([ev_dve[NF - 1]] if c == 0 else ())))
        for c in range(NF):
            em.wait(em.v, evs[c], ebc)
            i1 = em.v.tensor_tensor(tmp[...], xall[c][...], banks[6][...], ALU.mult)
            ev_t1[c] = t1 = em.ev(i1, "V")
            em.wait(em.v, t1)
            i2 = em.v.tensor_tensor(xln_tiles[c][...], tmp[...], banks[7][...], ALU.add)
            prev_tt = em.ev(i2, "V")
            ev_norm[c] = prev_tt
        pb.pop()
    else:
        for c in range(NF):
            e = em.dma(em.g, "DG", xrot[c % 3][...],
                       src[c * 128:(c + 1) * 128, s * TS:(s + 1) * TS],
                       waits=([ev_dve[NF - 1]] if c < 3 else [ev_norm[c - 3]]))
            em.wait(em.v, e, ebc)
            i1 = em.v.tensor_tensor(tmp[...], xrot[c % 3][...], banks[6][...], ALU.mult)
            ev_t1[c] = t1 = em.ev(i1, "V")
            em.wait(em.v, t1)
            i2 = em.v.tensor_tensor(xln_tiles[c][...], tmp[...], banks[7][...], ALU.add)
            prev_tt = em.ev(i2, "V")
            ev_norm[c] = prev_tt
    em.bfree[6] = prev_tt
    em.bfree[7] = prev_tt
    pb.pop(); pf.pop()
    return ev_norm



def prescale_B(em, st, row_dram, col0, nb, braw_dram, rawt, rrows, dstt, drows, bknum, prevevs):
    C, pf, banks = st["C"], st["pf"], st["banks"]
    e_raw = em.dma(em.g, "DG", rawt[rrows[0]:rrows[1], 0:nb],
                   braw_dram[0:32, col0:col0 + nb], waits=prevevs)
    srow = st["srow"]
    e_row = em.dma(em.g, "DG", srow[0:1, 0:nb], row_dram[0:1, col0:col0 + nb],
                   waits=prevevs)
    em.wait(em.v, e_row)
    e1 = em.ev(em.v.tensor_scalar(srow[0:1, 0:nb], srow[0:1, 0:nb], 0.005, 1e-25,
                                  ALU.mult, ALU.max), "V")
    em.wait(em.v, e1)
    e2 = em.ev(em.v.reciprocal(srow[0:1, 0:nb], srow[0:1, 0:nb]), "V")
    em.wait(em.v, e2)
    e3 = em.ev(em.v.tensor_scalar(srow[0:1, 0:nb], srow[0:1, 0:nb], 0.02, None,
                                  ALU.mult), "V")
    lastv = None
    for j0 in range(0, nb, 512):
        w = min(512, nb - j0)
        em.wait(em.t, e3, C["onescol_f"][1], em.bfree[bknum], lastv)
        i = em.t.matmul(banks[bknum].ap()[0:32, 0:w], C["onescol_f"][0][0:1, 0:32],
                        srow[0:1, j0:j0 + w], start=True, stop=True)
        emm = em.ev(i, "T")
        em.wait(em.v, emm, e_raw)
        iv = em.v.tensor_tensor(dstt[drows[0]:drows[1], j0:j0 + w],
                                rawt[rrows[0]:rrows[1], j0:j0 + w],
                                banks[bknum].ap()[0:32, 0:w], ALU.mult)
        lastv = em.ev(iv, "V")
    em.bfree[bknum] = lastv
    return lastv


def ffn_sublayer(em, st, src, dst, src_ready=None):
    C, pb, pf, banks, dr = st["C"], st["pb"], st["pf"], st["banks"], st["dr"]
    t0ev = ("T", em.sc["T"])
    v0ev = ("V", em.sc["V"])
    pb.push(); pf.push()
    xln = [pb.al(128, TS) for _ in range(NF)]
    h = [pb.al(128, TS) for _ in range(NI)]
    sg = [pb.al(128, TS) for _ in range(2)]
    uu = [pb.al(128, TS) for _ in range(2)]
    xabc = pb.al(128, TS)
    habc = pb.al(64, TS)
    xag, xau, xs = xabc[0:32, :], xabc[32:64, :], xabc[64:65, :]
    ha, hs = habc[0:32, :], habc[32:33, :]
    ones65 = pb.al(65, 128)
    wsh = pb.al(128, 11008)
    gdb = pb.al(64, 4096)
    rawb = pb.al(64, 1024)
    st["srow"] = pf.al(1, 1024)

    e_ones65 = em.ev(em.v.memset(ones65[...], 1.0), "V")
    ev_VH = [None] * NI
    ev_AG = [None] * NI
    ev_VU = [None] * NI
    ev_VY = [None] * NF
    ev_TD = [None] * NF
    for s in range(2):
        ev_xln = layer_norm(em, st, src, s, xln, [sg[0], sg[1], uu[0], uu[1]], src_ready)
        XA = banks[4].ap()[0:65, :]
        for c in range(NF):
            w = [C["aaug"][1], ev_xln[c]]
            if c == 0:
                w.append(em.bfree[4])
            em.wait(em.t, *w)
            i = em.t.matmul(XA, C["aaug"][0][:, c * 65:(c + 1) * 65], xln[c][...],
                            start=(c == 0), stop=(c == NF - 1))
        lastA = em.ev(i, "T")
        em.wait(em.a, lastA)
        em.ev(em.a.activation(xag[...], banks[4].ap()[0:32, :], AF.Copy, bias=0.0, scale=1.0), "A")
        em.ev(em.a.activation(xau[...], banks[4].ap()[32:64, :], AF.Copy, bias=0.0, scale=1.0), "A")
        ea3 = em.ev(em.a.activation(xs[...], banks[4].ap()[64:65, :], AF.Copy, bias=0.0, scale=-7.5), "A")
        em.bfree[4] = ea3

        ev_gub = [None] * 22
        last_ps = None
        ev_wgu = [None] * (2 * NI)
        ev_TG = [None] * NI
        ev_TU = [None] * NI
        for i_ in range(NI):
            if i_ % 8 == 0:
                grp = i_ // 8
                nb = min(1024, (NI - i_) * 128)
                w = [t0ev, v0ev]
                if grp >= 2:
                    w.append(ev_TU[(grp - 2) * 8 + 7] if (grp - 2) * 8 + 7 < NI else None)
                last_ps = prescale_B(em, st, dr["gsc_row"], i_ * 128, nb, dr["gmatB"],
                           rawb, (0, 32), gdb[0:64, (grp % 2) * 1024:(grp % 2) * 1024 + nb],
                           (0, 32), 4, tuple(w) + ((last_ps,) if last_ps else ()))
                ev_gub[grp] = last_ps = prescale_B(em, st, dr["usc_row"], i_ * 128, nb, dr["umatB"],
                                         rawb, (32, 64), gdb[0:64, (grp % 2) * 1024:(grp % 2) * 1024 + nb],
                                         (32, 64), 4, tuple(w) + ((last_ps,) if last_ps else ()))
            jg = 2 * i_
            ju = 2 * i_ + 1
            w = []
            if jg >= 2:
                w = [ev_TG[(jg - 2) // 2] if (jg - 2) % 2 == 0 else ev_TU[(jg - 2) // 2]]
            ev_wgu[jg] = em.dma(em.g, "DG", wsh[:, (jg % 2) * 4096:(jg % 2 + 1) * 4096],
                                dr["wguq8"][i_, :, 0:D], waits=w, max_dma_last_dim=2048)
            w = [t0ev]
            if ju >= 2:
                w = [ev_TG[(ju - 2) // 2] if (ju - 2) % 2 == 0 else ev_TU[(ju - 2) // 2]]
            ev_wgu[ju] = em.dma(em.g, "DG", wsh[:, (ju % 2) * 4096:(ju % 2 + 1) * 4096],
                                dr["wguq8"][i_, :, D:2 * D], waits=w, max_dma_last_dim=2048)
            G = banks[i_ % 2]
            U = banks[2 + i_ % 2]
            grp = i_ // 8
            wt = [ev_wgu[jg], ea3, ev_gub[grp], e_ones65]
            if i_ >= 2:
                wt += [ev_AG[i_ - 2], ev_VU[i_ - 2]]
            else:
                wt += [em.bfree[i_ % 2], em.bfree[2 + i_ % 2]]
            em.wait(em.t, *wt)
            gb = wsh[:, (jg % 2) * 4096:(jg % 2 + 1) * 4096]
            ub = wsh[:, (ju % 2) * 4096:(ju % 2 + 1) * 4096]
            em.t.matmul(G[...], gdb[0:32, (grp % 2) * 1024 + (i_ % 8) * 128:(grp % 2) * 1024 + (i_ % 8 + 1) * 128],
                        xag[...], start=True, stop=False)
            for c in range(16):
                em.t.matmul(G[...], gb[:, c * 128:(c + 1) * 128], xln[c][...],
                            start=False, stop=False)
            em.t.matmul(G[...], ones65[64:65, :], xs[...], start=False, stop=False)
            for c in range(16, NF):
                i = em.t.matmul(G[...], gb[:, c * 128:(c + 1) * 128], xln[c][...],
                                start=False, stop=(c == NF - 1))
            ev_TG[i_] = em.ev(i, "T")
            em.wait(em.t, ev_wgu[ju])
            em.t.matmul(U[...], gdb[32:64, (grp % 2) * 1024 + (i_ % 8) * 128:(grp % 2) * 1024 + (i_ % 8 + 1) * 128],
                        xau[...], start=True, stop=False)
            for c in range(16):
                em.t.matmul(U[...], ub[:, c * 128:(c + 1) * 128], xln[c][...],
                            start=False, stop=False)
            em.t.matmul(U[...], ones65[64:65, :], xs[...], start=False, stop=False)
            for c in range(16, NF):
                i = em.t.matmul(U[...], ub[:, c * 128:(c + 1) * 128], xln[c][...],
                                start=False, stop=(c == NF - 1))
            ev_TU[i_] = em.ev(i, "T")
            w = [ev_TG[i_], C["scg"][1]]
            if i_ >= 2:
                w.append(ev_VH[i_ - 2])
            em.wait(em.a, *w)
            ia = em.a.activation(sg[i_ % 2][...], G[...], AF.Silu, bias=0.0,
                                 scale=C["scg"][0][:, i_:i_ + 1])
            ev_AG[i_] = em.ev(ia, "A")
            em.wait(em.v, ev_TU[i_], C["scu"][1])
            iv = em.v.tensor_scalar(uu[i_ % 2][...], U[...], C["scu"][0][:, i_:i_ + 1],
                                    None, ALU.mult)
            ev_VU[i_] = em.ev(iv, "V")
            em.wait(em.v, ev_AG[i_], ev_VU[i_], t0ev)
            iv = em.v.tensor_tensor(h[i_][...], sg[i_ % 2][...], uu[i_ % 2][...], ALU.mult)
            ev_VH[i_] = em.ev(iv, "V")

        HA = banks[4].ap()[0:33, :]
        for ic in range(NI):
            w = [C["daug"][1], ev_VH[ic]]
            if ic == 0:
                w.append(em.bfree[4])
            em.wait(em.t, *w)
            i = em.t.matmul(HA, C["daug"][0][:, ic * 33:(ic + 1) * 33], h[ic][...],
                            start=(ic == 0), stop=(ic == NI - 1))
        lastHA = em.ev(i, "T")
        em.wait(em.a, lastHA)
        em.ev(em.a.activation(ha[...], banks[4].ap()[0:32, :], AF.Copy, bias=0.0, scale=1.0), "A")
        eh2 = em.ev(em.a.activation(hs[...], banks[4].ap()[32:33, :], AF.Copy, bias=0.0, scale=-7.5), "A")
        
        em.bfree[4] = eh2

        e_db = None
        for q0 in range(0, D, 1024):
            e_db = prescale_B(em, st, dr["dsc_row"], q0, 1024, dr["dmatB"],
                              rawb, (0, 32), gdb[0:64, q0:q0 + 1024], (0, 32), 4,
                              (ev_TU[NI - 1],) + ((e_db,) if e_db else ()) + ((last_ps,) if last_ps else ()))

        pf.push()
        y0 = [pf.al(128, TS) for _ in range(2)]
        xo = [pf.al(128, TS) for _ in range(2)]
        ob = [pf.al(128, TS) for _ in range(2)]
        ev_wd = [None] * (2 * NF)
        ev_TDh = [None] * NF
        ev_VO = [None] * NF
        for o in range(NF):
            for half in range(2):
                j = 2 * o + half
                w = []
                if o >= 1:
                    w.append(ev_TDh[o - 1] if half == 0 else ev_TD[o - 1])
                else:
                    w.append(ev_TU[NI - 1])
                ev_wd[j] = em.dma(em.g, "DG", wsh[:, (half) * 5504:(half + 1) * 5504],
                                  dr["wdq8"][o, :, half * 5504:(half + 1) * 5504],
                                  waits=w, max_dma_last_dim=2048)
            e_xo = em.dma(em.g, "DG", xo[o % 2][...],
                          src[o * 128:(o + 1) * 128, s * TS:(s + 1) * TS],
                          waits=([ev_VO[o - 2], src_ready] if o >= 2 else [src_ready]))
            DD = banks[o % 2]
            wt = [ev_wd[2 * o], eh2, e_db]
            if o >= 2:
                wt.append(ev_VY[o - 2])
            else:
                wt += [ev_AG[NI - 1], ev_VU[NI - 1]]
            em.wait(em.t, *wt)
            em.t.matmul(DD[...], gdb[0:32, o * 128:(o + 1) * 128], ha[...],
                        start=True, stop=False)
            for ic in range(43):
                i = em.t.matmul(DD[...], wsh[:, ic * 128:(ic + 1) * 128],
                                h[ic][...], start=False, stop=False)
            ev_TDh[o] = em.ev(i, "T")
            em.wait(em.t, ev_wd[2 * o + 1])
            em.t.matmul(DD[...], ones65[32:33, :], hs[...], start=False, stop=False)
            for ic in range(43):
                i = em.t.matmul(DD[...], wsh[:, 5504 + ic * 128:5504 + (ic + 1) * 128],
                                h[43 + ic][...], start=False, stop=(ic == 42))
            ev_TD[o] = em.ev(i, "T")
            em.wait(em.v, ev_TD[o], C["scd05"][1])
            iv = em.v.tensor_scalar(y0[o % 2][...], DD[...], C["scd05"][0][:, o:o + 1],
                                    None, ALU.mult)
            ev_VY[o] = em.ev(iv, "V")
            em.wait(em.v, ev_VY[o], e_xo)
            iv = em.v.tensor_tensor(ob[o % 2][...], y0[o % 2][...], xo[o % 2][...], ALU.add)
            ev_VO[o] = em.ev(iv, "V")
            em.dma(em.g, "DG", dst[o * 128:(o + 1) * 128, s * TS:(s + 1) * TS], ob[o % 2][...],
                   waits=(ev_VO[o],))
        pf.pop()
        em.bfree[0] = ev_VY[NF - 2]
        em.bfree[1] = ev_VY[NF - 1]
        em.bfree[2] = ev_VU[NI - 2]
        em.bfree[3] = ev_VU[NI - 1]
    pb.pop(); pf.pop()


def attention(em, st, src_ready=None):
    C, pb, pf, banks, dr = st["C"], st["pb"], st["pf"], st["banks"], st["dr"]
    t0ev = ("T", em.sc["T"])
    v0ev = ("V", em.sc["V"])
    pb.push(); pf.push()

    arena = [pb.al(128, TOK) for _ in range(32)]
    wqb = [pb.al(128, D) for _ in range(2)]
    ev_wq = [None] * 80
    ev_qkvT = [None] * 80
    ev_arena = [None] * 32
    ev_kv = [None] * 16
    ev_x2 = [[None] * NF, [None] * NF]

    def qkv_block(ob, wsrc, dst_tile, wq_waits, extra_mm_waits=()):
        ev_wq[ob] = em.dma(em.g, "DG", wqb[ob % 2][...], wsrc,
                           waits=wq_waits, max_dma_last_dim=2048)
        out_evs = []
        for half in range(2):
            bk = banks[ob % 2]
            for c in range(NF):
                w = [ev_x2[half][c]]
                if c == 0:
                    w += [ev_wq[ob], em.bfree[ob % 2]] + list(extra_mm_waits)
                em.wait(em.t, *w)
                i = em.t.matmul(bk[...], wqb[ob % 2][:, c * 128:(c + 1) * 128],
                                xln2[c][:, half * TS:(half + 1) * TS],
                                start=(c == 0), stop=(c == NF - 1))
            ei = em.ev(i, "T")
            ev_qkvT[ob] = ei
            em.wait(em.v, ei)
            iv = em.v.tensor_copy(dst_tile[:, half * TS:(half + 1) * TS], bk[...])
            ev = em.ev(iv, "V")
            em.bfree[ob % 2] = ev
            out_evs.append(ev)
        return out_evs[-1]

    def transpose128(src_ap, dst_ap, ready_evs, war_ev=None):
        ro = src_ap.shape[1]
        co = src_ap.shape[0]
        em.wait(em.t, C["ident"][1], em.bfree[7], *ready_evs)
        nin = src_ap.shape[0]
        i = em.t.transpose(banks[7].ap().bitcast(BF16)[0:ro, 0:co],
                           src_ap, C["ident"][0][0:nin, 0:nin])
        et = em.ev(i, "T")
        em.wait(em.v, et, war_ev)
        iv = em.v.tensor_copy(dst_ap, banks[7].ap().bitcast(BF16)[0:ro, 0:co])
        ev = em.ev(iv, "V")
        em.bfree[7] = ev
        return ev

    pb.push()
    wgt = pb.al(128, NF * 96)
    gates = pb.al(96, TOK)
    e_wgt = None
    for c in range(NF):
        e_wgt = em.dma(em.g, "DG", wgt[:, c * 96:(c + 1) * 96], dr["wgT"][c],
                       waits=((t0ev,) if c == 0 else ()))

    pb.push()
    xln2 = [pb.al(128, TOK) for _ in range(NF)]
    lnscr = [pb.al(128, TS) for _ in range(4)]
    rdy = src_ready
    for s in range(2):
        evs = layer_norm(em, st, dr["xres1"], s,
                         [t[:, s * TS:(s + 1) * TS] for t in xln2], lnscr,
                         src_ready=rdy)
        for c in range(NF):
            ev_x2[s][c] = evs[c]
        rdy = evs[NF - 1]
    ev_gates = None
    for half in range(2):
        for c in range(NF):
            w = [e_wgt, ev_x2[half][c]]
            if c == 0:
                w.append(em.bfree[5])
            em.wait(em.t, *w)
            i = em.t.matmul(banks[5].ap()[0:96, :], wgt[:, c * 96:(c + 1) * 96],
                            xln2[c][:, half * TS:(half + 1) * TS],
                            start=(c == 0), stop=(c == NF - 1))
        eg = em.ev(i, "T")
        em.wait(em.a, eg, C["bg"][1])
        ia = em.a.activation(gates[:, half * TS:(half + 1) * TS], banks[5].ap()[0:96, :],
                             AF.Sigmoid, bias=C["bg"][0][...], scale=1.0)
        ev_gates = em.ev(ia, "A")
        em.bfree[5] = ev_gates
    for ob in range(32, 48):
        h = ob - 32
        w = [ev_qkvT[ob - 2]] if ob >= 34 else [t0ev]
        ev_kv[h] = qkv_block(ob, dr["wqkvT"][ob], arena[h], w)
    for c in range(NF):
        em.dma(em.g, "DG", dr["xlnsp"][c * 128:(c + 1) * 128, :], xln2[c][...],
               waits=(ev_x2[0][c], ev_x2[1][c]))
    spill_fence = em.fence("DG")
    if STOP_AFTER == "XLN2A":
        for c in range(NF):
            em.dma(em.g, "DG", dr["yout"][c * 128:(c + 1) * 128, :], xln2[c][...],
                   waits=(ev_x2[0][c], ev_x2[1][c]))
        pb.pop()
        pb.pop()
        pb.pop()
        return em.fence("DG") + em.fence("DY")
    pb.pop()

    pb.push()
    rcs = pb.al(128, TOK)
    rsn = pb.al(128, TOK)
    em.dma(em.g, "DG", rcs[...], dr["rcos"], waits=(spill_fence,))
    e_rs = em.dma(em.g, "DG", rsn[...], dr["rsin"], waits=(spill_fence,))
    kposb = [pb.al(128, TOK) for _ in range(2)]
    rkt = [pb.al(128, TOK) for _ in range(2)]
    kkp = [pb.al(128, TOK) for _ in range(2)]
    vtt = [pb.al(128, 128) for _ in range(2)]
    swsc = pb.al(128, TS)
    ckin = [[pb.al(128, 64) for _ in range(16)] for _ in range(8)]
    ck1 = [[pb.al(128, 64) for _ in range(16)] for _ in range(8)]
    w1b = [pb.al(128, CD) for _ in range(2)]
    w2t = pb.al(128, 16 * 128)
    cmy = pb.al(128, 64)
    cvt = pb.al(64, 128)

    def rope(src_tile, dst_tile, ready_ev, scratch_ev):
        last = None
        for half in range(2):
            sl = slice(half * TS, (half + 1) * TS)
            em.wait(em.t, ready_ev, C["swapm"][1], em.bfree[6],
                    scratch_ev if half == 0 else last)
            i = em.t.matmul(banks[6][...], C["swapm"][0][...], src_tile[:, sl],
                            start=True, stop=True)
            eswap = em.ev(i, "T")
            em.wait(em.v, eswap, e_rs)
            i1 = em.v.tensor_tensor(swsc[...], banks[6][...], rsn[:, sl], ALU.mult)
            e1 = em.ev(i1, "V")
            em.bfree[6] = e1
            em.wait(em.v, e1)
            i2 = em.v.tensor_tensor(dst_tile[:, sl], src_tile[:, sl], rcs[:, sl], ALU.mult)
            e2 = em.ev(i2, "V")
            em.wait(em.v, e2)
            i3 = em.v.tensor_tensor(dst_tile[:, sl], dst_tile[:, sl], swsc[...], ALU.add)
            last = em.ev(i3, "V")
        return last

    ev_ckin = [[None] * 16 for _ in range(8)]
    ev_rkt = [None] * 8
    ev_vc = {}
    e_rc_all = [None] * 8
    prev_rope = None
    prev_kkp = None
    for h in range(8):
        e_kp = em.dma(em.g, "DG", kposb[h % 2][...], dr["kpos"][h],
                      waits=([ev_ckin[h - 2][15]] if h >= 2 else [v0ev, spill_fence]))
        ev_rkt[h] = rope(arena[h], rkt[h % 2], ev_kv[h],
                         (e_rc_all[h - 2] if h >= 2 else spill_fence))
        e_rc = None
        for s in range(2):
            e_rc = em.dma(em.g, "DG", dr["ctr_rk"][s, h * 128:(h + 1) * 128, :],
                          rkt[h % 2][:, s * TS:(s + 1) * TS], waits=(ev_rkt[h],))
        e_rc_all[h] = e_rc
        for j in range(8):
            ev = transpose128(arena[8 + h][:, j * 128:(j + 1) * 128], vtt[j % 2][...],
                              (ev_kv[8 + h], spill_fence),
                              war_ev=ev_vc.get((h, j - 2), ev_vc.get((h - 1, 6 + j))))
            s, jj = j // 4, j % 4
            ev_vc[(h, j)] = em.dma(em.g, "DG", dr["ctr_v"][s, h, jj * 128:(jj + 1) * 128, :],
                                   vtt[j % 2][...], waits=(ev,))
        em.wait(em.v, e_kp, ev_kv[h], prev_kkp, spill_fence)
        ik = em.v.tensor_tensor(kkp[h % 2][...], arena[h][...], kposb[h % 2][...], ALU.add)
        ekk = em.ev(ik, "V")
        prev_kkp = ekk
        for j in range(16):
            em.wait(em.v, ekk)
            iv = em.v.tensor_copy(ckin[h][j][...], kkp[h % 2][:, j::16])
            ev_ckin[h][j] = em.ev(iv, "V")

    def compress_side(w1_dram, w2_dram, b1t, b2t, dstname):
        e_w2 = None
        for c2 in range(16):
            e_w2 = em.dma(em.g, "DG", w2t[:, c2 * 128:(c2 + 1) * 128], w2_dram[c2],
                          waits=((spill_fence,) if c2 == 0 else ()))
        ev_w1 = [None] * 16
        ev_ck1 = [[None] * 16 for _ in range(8)]
        for ob in range(16):
            w = [("T", em.sc["T"]), spill_fence] if ob < 2 else [ev_ck1[7][ob - 2]]
            ev_w1[ob] = em.dma(em.g, "DG", w1b[ob % 2][...], w1_dram[ob],
                               waits=w, max_dma_last_dim=2048)
            for h in range(8):
                em.wait(em.t, ev_w1[ob], em.bfree[6])
                for jc in range(16):
                    em.wait(em.t, ev_ckin[h][jc])
                    i = em.t.matmul(banks[6].ap()[:, 0:64],
                                    w1b[ob % 2][:, jc * 128:(jc + 1) * 128],
                                    ckin[h][jc][...], start=(jc == 0), stop=(jc == 15))
                ei = em.ev(i, "T")
                em.wait(em.a, ei, b1t[1], spill_fence)
                ia = em.a.activation(ck1[h][ob][...], banks[6].ap()[:, 0:64], AF.Relu,
                                     bias=b1t[0][:, ob:ob + 1], scale=1.0)
                ev_ck1[h][ob] = em.ev(ia, "A")
                em.bfree[6] = ev_ck1[h][ob]
        prev_c = None
        for h in range(8):
            em.wait(em.t, e_w2, em.bfree[6])
            for c2 in range(16):
                em.wait(em.t, ev_ck1[h][c2])
                i = em.t.matmul(banks[6].ap()[:, 0:64], w2t[:, c2 * 128:(c2 + 1) * 128],
                                ck1[h][c2][...], start=(c2 == 0), stop=(c2 == 15))
            ei = em.ev(i, "T")
            em.wait(em.v, ei, b2t[1], prev_c)
            iv = em.v.tensor_scalar(cmy[...], banks[6].ap()[:, 0:64], b2t[0][...],
                                    None, ALU.add)
            ev = em.ev(iv, "V")
            em.bfree[6] = ev
            if dstname == "ctr_ck":
                for s in range(2):
                    prev_c = em.dma(em.g, "DG", dr["ctr_ck"][s, h],
                                    cmy[:, s * 32:(s + 1) * 32], waits=(ev,))
            else:
                ev2 = transpose128(cmy[...], cvt[...], (ev,))
                prev_c = ev2
                for s in range(2):
                    prev_c = em.dma(em.g, "DG", dr["ctr_cv"][s, h],
                                    cvt[s * 32:(s + 1) * 32, :], waits=(ev2,))

    compress_side(dr["wk1T"], dr["wk2T"], C["kb1"], C["kb2"], "ctr_ck")
    prev_kkp2 = None
    for h in range(8):
        e_vp = em.dma(em.g, "DG", kposb[h % 2][...], dr["vpos"][h],
                      waits=([ev_ckin[h - 2][15]] if h >= 2 else
                             [("T", em.sc["T"])]))
        em.wait(em.v, e_vp, ev_kv[8 + h], prev_kkp2, ev_ckin[h][15])
        ik = em.v.tensor_tensor(kkp[h % 2][...], arena[8 + h][...], kposb[h % 2][...], ALU.add)
        ekk = em.ev(ik, "V")
        prev_kkp2 = ekk
        for j in range(16):
            em.wait(em.v, ekk, ("T", em.sc["T"]) if j == 0 else None)
            iv = em.v.tensor_copy(ckin[h][j][...], kkp[h % 2][:, j::16])
            ev_ckin[h][j] = em.ev(iv, "V")
    compress_side(dr["wv1T"], dr["wv2T"], C["vb1"], C["vb2"], "ctr_cv")
    pb.pop()

    em.wait(em.g, em.fence("DG"))
    ecc = []
    for nm_i, nm_o in (("ctr_rk", "gat_rk"), ("ctr_v", "gat_v"),
                       ("ctr_ck", "gat_ck"), ("ctr_cv", "gat_cv")):
        ai = dr[nm_i]
        ao = dr[nm_o]
        ai2 = ai.rearrange("a b c -> (a b) c") if ai.ndim == 3 else ai.rearrange("a b c d -> (a b c) d")
        ao2 = ao.rearrange("a b c -> (a b) c") if ao.ndim == 3 else ao.rearrange("a b c d -> (a b c) d")
        i = em.g.collective_compute("AllGather", ALU.bypass,
                                    replica_groups=[[0, 1, 2, 3], [4, 5, 6, 7]],
                                    ins=[ai2.opt()], outs=[ao2.opt()])
        ecc.append(em.ev(i, "CC"))
    cc_done = ecc[-1]

    pb.push()
    xln2 = [pb.al(128, TOK) for _ in range(NF)]
    qfence = em.fence("DG") + (("T", em.sc["T"]), ("V", em.sc["V"]), ("A", em.sc["A"]))
    for c in range(NF):
        e_xr = em.dma(em.g, "DG", xln2[c][...], dr["xlnsp"][c * 128:(c + 1) * 128, :],
                      waits=(qfence if c == 0 else ()))
        ev_x2[0][c] = ev_x2[1][c] = e_xr
    if STOP_AFTER == "XLN2":
        for c in range(NF):
            em.dma(em.g, "DG", dr["yout"][c * 128:(c + 1) * 128, :], xln2[c][...],
                   waits=(ev_x2[0][c],))
        pb.pop()
        pb.pop()
        return em.fence("DG") + em.fence("DY")
    for ob in range(32):
        w = [ev_qkvT[ob - 2]] if ob >= 2 else [ev_qkvT[47]]
        extra = ()
        if ob < 16:
            extra = (("G", em.sc["G"]),)
            w = w + [em.fence("DG")]
        ev_arena[ob] = qkv_block(ob, dr["wqkvT"][ob], arena[ob], w, extra)
    gcol = pf.al(128, 8 * 96)
    for jj in range(8):
        transpose128(gates[:, jj * 128:(jj + 1) * 128], gcol[:, jj * 96:(jj + 1) * 96],
                     (ev_gates,))
    ev_gcol = ("V", em.sc["V"])
    pb.pop()
    pb.pop()

    if STOP_AFTER == "QKV":
        for c in range(32):
            em.dma(em.g, "DG", dr["yout"][c * 128:(c + 1) * 128, :], arena[c][...],
                   waits=(ev_arena[c],))
        pb.pop()
        return em.fence("DG") + em.fence("DY")
    pb.push()
    mskc = pb.al(128, 6 * TS)
    mskl = pb.al(128, 10 * TS)
    mski = pb.al(128, 8 * 384)
    eex = pb.al(128, 16 * 128)
    memvt = pb.al(1, KVH * 128)
    kvfence = em.fence("DG") + (("T", em.sc["T"]), ("V", em.sc["V"]), ("A", em.sc["A"]))
    for i6 in range(6):
        em.dma(em.g, "DG", mskc[:, i6 * TS:(i6 + 1) * TS],
               dr["maskc"].rearrange("s k p f -> (s k) p f")[i6],
               waits=(kvfence if i6 == 0 else ()))
    for i10 in range(10):
        em.dma(em.g, "DG", mskl[:, i10 * TS:(i10 + 1) * TS],
               dr["slidem"].rearrange("s k p f -> (s k) p f")[i10])
    for i8 in range(8):
        e_m3 = em.dma(em.g, "DG", mski[:, i8 * 384:(i8 + 1) * 384],
                      dr["maskci"].rearrange("s j p f -> (s j) p f")[i8])
    for i16 in range(16):
        e_ee = em.dma(em.g, "DG", eex[:, i16 * 128:(i16 + 1) * 128], dr["eexp"][i16])
    e_mv = em.dma(em.g, "DG", memvt[...], dr["memv"].rearrange("h o d -> o (h d)"))
    maskf = em.fence("DG")

    predsb = pf.al(1, 2)
    e_pr = em.dma(em.g, "DG", predsb[...].bitcast(I32), dr["preds"])
    regA = em.g.register("ipA").__enter__()
    regB = em.g.register("ipB").__enter__()
    em.wait(em.g, e_pr)
    em.g.reg_load(regA, predsb[...].bitcast(I32)[0:1, 0:1])
    em.g.reg_load(regB, predsb[...].bitcast(I32)[0:1, 1:2])
    ipAv = em.g.snap(regA)
    ipBv = em.g.snap(regB)
    em.wait(em.g, cc_done)
    slpf = []
    for s_, ipv_ in ((0, ipAv), (1, ipBv)):
        slpf.append(em.dma(em.g, "DG", dr["slp_rk"][s_],
                           dr["gat_rk"][bass.ds(ipv_, 1), :, 384:512]))
        slpf.append(em.dma(em.g, "DG", dr["slp_v"][s_],
                           dr["gat_v"][bass.ds(ipv_, 1), :, 384:512, :]))

    rcs2 = pb.al(128, TOK)
    rsn2 = pb.al(128, TOK)
    sws2 = pb.al(128, TS)
    em.dma(em.g, "DG", rcs2[...], dr["rcos"], waits=(kvfence,))
    e_rs2 = em.dma(em.g, "DG", rsn2[...], dr["rsin"])
    rkfm = pb.al(128, 4096)
    vtm = pb.al(128, 4096)
    ckf = pb.al(128, 384)
    cvt2 = pb.al(128, 384)
    slK = pb.al(128, 640)
    slV = pb.al(128, 640)
    RQ = [pb.al(128, TOK) for _ in range(4)]
    qm = pb.al(128, TOK)
    selTa = pb.al(128, TS)
    selTb = pb.al(128, TS)
    sel01 = [pb.al(128, TS) for _ in range(2)]
    pt = [pb.al(128, TS) for _ in range(2)]
    ptm = [pb.al(128, TS) for _ in range(2)]
    otmp = pb.al(128, 128)
    impes = pf.al(128, 384)
    impm = pf.al(128, 384)
    tmpv = pf.al(128, 384)
    tmpe = pf.al(128, 128)
    otj = [pf.al(128, TS) for _ in range(4)]
    rzal = pf.al(128, 16)
    gsel = pf.al(128, 16)
    coefa = pf.al(128, 16)
    t4c = pf.al(128, 1)
    thrc = pf.al(128, 1)
    zc = pf.al(128, 1)
    mx = pf.al(128, 1)

    ZB = banks[5]
    SIM = banks[6]
    ev_attnout = [None] * 32

    def flat(g):
        return (g * 2 + 1) if g < 4 else ((7 - g) * 2)

    def load_kvh(h, wprev):
        evs = []
        for g in range(8):
            evs.append(em.dma(em.g, "DG", rkfm[:, g * TS:(g + 1) * TS],
                              dr["gat_rk"][flat(g), h * 128:(h + 1) * 128, :],
                              waits=([cc_done, wprev, kvfence] if g == 0 else ())))
            for q4 in range(4):
                kc = g * 4 + q4
                evs.append(em.dma(em.g, "DG", vtm[:, kc * 128:(kc + 1) * 128],
                                  dr["gat_v"][flat(g), h, q4 * 128:(q4 + 1) * 128, :]))
        evs.append(em.dma(em.g, "DG", ckf[:, 0:1], dr["memk"][h]))
        for g in range(8):
            evs.append(em.dma(em.g, "DG", ckf[:, 1 + g * 32:1 + (g + 1) * 32],
                              dr["gat_ck"][flat(g), h]))
        iz = em.v.memset(ckf[:, 257:384], 0.0)
        em.wait(em.v, wprev)
        ez1 = em.ev(iz, "V")
        iz = em.v.memset(cvt2[:, 256:384], 0.0)
        ez2 = em.ev(iz, "V")
        em.wait(em.v, e_mv, ez2)
        iv = em.v.tensor_copy(cvt2[0:1, 0:128], memvt[0:1, h * 128:(h + 1) * 128])
        ez3 = em.ev(iv, "V")
        evs.append(ez1)
        evs.append(ez3)
        for g in range(8):
            k0 = 1 + g * 32
            k1 = k0 + 32
            kk = k0
            while kk < k1:
                t_ = kk // 128
                hi = min(k1, (t_ + 1) * 128)
                evs.append(em.dma(em.g, "DG",
                                  cvt2[kk - t_ * 128:hi - t_ * 128, t_ * 128:(t_ + 1) * 128],
                                  dr["gat_cv"][flat(g), h, kk - k0:hi - k0, :],
                                  waits=(ez2, ez3)))
                kk = hi
        return evs

    def load_slide(h, s, wprev):
        evs = [em.dma(em.g, "DG", slK[:, 0:128],
                      dr["slp_rk"][s, h * 128:(h + 1) * 128, :],
                      waits=tuple(slpf) + ((wprev,) if wprev else ())),
               em.dma(em.g, "DG", slK[:, 128:640],
                      dr["ctr_rk"][s, h * 128:(h + 1) * 128, :]),
               em.dma(em.g, "DG", slV[:, 0:128], dr["slp_v"][s, h])]
        for jj in range(4):
            evs.append(em.dma(em.g, "DG", slV[:, (1 + jj) * 128:(2 + jj) * 128],
                              dr["ctr_v"][s, h, jj * 128:(jj + 1) * 128, :]))
        return evs

    lds = load_kvh(0, kvfence)
    for h in range(KVH):
        em.wait(em.v, ev_arena[4 * h], ev_arena[4 * h + 1], ev_arena[4 * h + 2],
                ev_arena[4 * h + 3])
        i = em.v.tensor_tensor(qm[...], arena[4 * h][...], arena[4 * h + 1][...], ALU.add)
        e = em.ev(i, "V")
        em.wait(em.v, e)
        i = em.v.tensor_tensor(qm[...], qm[...], arena[4 * h + 2][...], ALU.add)
        e = em.ev(i, "V")
        em.wait(em.v, e)
        i = em.v.tensor_tensor(qm[...], qm[...], arena[4 * h + 3][...], ALU.add)
        e_qm = em.ev(i, "V")
        prev_r = None
        ev_rq = [None] * 4
        for qh in range(4):
            ev_rq[qh] = rope2(em, C, banks, RQ, arena, h, qh, ev_arena[4 * h + qh],
                              prev_r, rcs2, rsn2, sws2, e_rs2)
            prev_r = ev_rq[qh]
        unit = [0]
        eA = {}
        eP = {}
        last_pe = [None]

        def pexp_mask(simtile_cols, mask_ap, scale):
            n = unit[0]
            unit[0] += 1
            rot = n % 2
            em.wait(em.a, last_pe[0], C["m48"][1], eP.get(n - 2))
            ia = em.a.activation(pt[rot][:, 0:simtile_cols], SIM.ap()[:, 0:simtile_cols],
                                 AF.Exp, bias=C["m48"][0][...], scale=scale)
            eA[n] = em.ev(ia, "A")
            em.bfree[6] = eA[n]
            em.wait(em.g, eA[n], maskf, eP.get(n - 2))
            ig = em.g.tensor_tensor(ptm[rot][:, 0:simtile_cols], pt[rot][:, 0:simtile_cols],
                                    mask_ap, ALU.mult)
            eP[n] = em.ev(ig, "G")
            return rot, eP[n]

        if STOP_AFTER == "SLKD":
            esl0 = load_slide(h, 0, last_pe[0] or ("T", em.sc["T"]))
            ed1 = em.dma(em.g, "DG", dr["yout"][h * 128:(h + 1) * 128, 0:640], slK[...],
                         waits=tuple(esl0))
            ed2 = em.dma(em.g, "DG", dr["yout"][(h + 8) * 128:(h + 9) * 128, 0:640],
                         slV.transpose() if False else slV[...], waits=tuple(esl0) + (ed1,))
            em.wait(em.g, ed2)
            continue
        if STOP_AFTER == "RQD":
            for qh in range(4):
                em.dma(em.g, "DG", dr["yout"][(4 * h + qh) * 128:(4 * h + qh + 1) * 128, :],
                       RQ[qh][...], waits=(ev_rq[qh],))
            em.wait(em.v, em.fence("DG"))
            continue
        prev_slot_fence = None
        for s in range(2):
            esl = load_slide(h, s, last_pe[0] or ("T", em.sc["T"]))
            ev_selT = None
            for jt in range(4):
                w = [e_qm, em.bfree[4]]
                if jt == 0:
                    w += lds
                em.wait(em.t, *w)
                i = em.t.matmul(banks[4].ap()[:, 0:384],
                                qm[:, s * TS + jt * 128:s * TS + (jt + 1) * 128],
                                ckf[...], start=True, stop=True)
                ei = em.ev(i, "T")
                em.wait(em.a, ei, C["m48"][1])
                ia = em.a.activation(impes[...], banks[4].ap()[:, 0:384], AF.Exp,
                                     bias=C["m48"][0][...], scale=SCALE * 0.25)
                ea = em.ev(ia, "A")
                em.bfree[4] = ea
                em.wait(em.v, ea, e_m3)
                iv = em.v.tensor_tensor(impm[...], impes[...],
                                        mski[:, (s * 4 + jt) * 384:(s * 4 + jt + 1) * 384],
                                        ALU.mult)
                ev = em.ev(iv, "V")
                em.wait(em.v, ev)
                iv = em.v.reduce_sum(zc[...], impm[...], axis=AX.X)
                ev = em.ev(iv, "V")
                em.wait(em.v, ev)
                iv = em.v.tensor_scalar(thrc[...], zc[...], 1e-10, None, ALU.mult)
                ev = em.ev(iv, "V")
                em.wait(em.v, ev)
                iv = em.v.tensor_copy(tmpv[...], impm[...])
                ev = em.ev(iv, "V")
                for it in range(4):
                    em.wait(em.v, ev)
                    iv = em.v.reduce_max(t4c[...] if it == 3 else mx[...], tmpv[...], axis=AX.X)
                    ev = em.ev(iv, "V")
                    if it < 3:
                        em.wait(em.v, ev)
                        iv = em.v.tensor_scalar(impes[...], tmpv[...], mx[...], -1e30,
                                                ALU.is_ge, ALU.mult)
                        ev = em.ev(iv, "V")
                        em.wait(em.v, ev)
                        iv = em.v.tensor_tensor(tmpv[...], tmpv[...], impes[...], ALU.add)
                        ev = em.ev(iv, "V")
                em.wait(em.v, ev)
                iv = em.v.tensor_scalar(impes[...], impm[...], t4c[...], None, ALU.is_ge)
                ev = em.ev(iv, "V")
                em.wait(em.v, ev)
                iv = em.v.tensor_scalar(impm[...], impm[...], thrc[...], None, ALU.is_gt)
                ev = em.ev(iv, "V")
                em.wait(em.v, ev)
                iv = em.v.tensor_tensor(pt[0][:, 0:384], impes[...], impm[...], ALU.mult)
                ev = em.ev(iv, "V")
                for bt in range(2):
                    dstT = selTa if bt == 0 else selTb
                    ev_selT = transpose128(pt[0][:, 1 + bt * 128:1 + (bt + 1) * 128],
                                           dstT[:, jt * 128:(jt + 1) * 128], (ev,))
            def evac(pass_idx, branch, last_pv_ev):
                em.wait(em.v, last_pv_ev, ev_gcol)
                iv = em.v.reciprocal(rzal[...], ZB.ap()[:, 0:16])
                ev = em.ev(iv, "V")
                em.bfree[5] = ev
                for j in range(4):
                    em.wait(em.v, ev)
                    base = (s * 4 + j) * 96 + 3 * (4 * h) + branch
                    iv = em.v.tensor_copy(gsel[:, j:13 + j:4], gcol[:, base:base + 10:3])
                    ev = em.ev(iv, "V")
                em.wait(em.v, ev)
                iv = em.v.tensor_tensor(coefa[...], gsel[...], rzal[...], ALU.mult)
                ev = em.ev(iv, "V")
                evs_done = []
                for qh in range(4):
                    for j in range(4):
                        col = qh * 4 + j
                        em.wait(em.v, ev)
                        if pass_idx == 0:
                            iv = em.v.tensor_scalar(otj[qh][:, j * 128:(j + 1) * 128],
                                                    banks[qh][:, j * 128:(j + 1) * 128],
                                                    coefa[:, col:col + 1], None, ALU.mult)
                            e2 = em.ev(iv, "V")
                        else:
                            iv = em.v.tensor_scalar(tmpe[...],
                                                    banks[qh][:, j * 128:(j + 1) * 128],
                                                    coefa[:, col:col + 1], None, ALU.mult)
                            e1 = em.ev(iv, "V")
                            em.wait(em.v, e1)
                            iv = em.v.tensor_tensor(otj[qh][:, j * 128:(j + 1) * 128],
                                                    otj[qh][:, j * 128:(j + 1) * 128],
                                                    tmpe[...], ALU.add)
                            e2 = em.ev(iv, "V")
                        evs_done.append(e2)
                    em.bfree[qh] = evs_done[-1]
                return evs_done[-1]

            first_z = [True]

            def pv_z(ptile_rot, kc_rhs_v, qh, start_bank):
                last = None
                for j in range(4):
                    if first_z[0]:
                        em.wait(em.t, em.bfree[5])
                    em.t.matmul(banks[qh][:, j * 128:(j + 1) * 128],
                                ptm[ptile_rot][:, j * 128:(j + 1) * 128],
                                kc_rhs_v, start=(start_bank and j == 0),
                                stop=False)
                    i = em.t.matmul(ZB.ap()[:, qh * 4 + j:qh * 4 + j + 1],
                                    ptm[ptile_rot][:, j * 128:(j + 1) * 128],
                                    C["onescol_b"][0][...],
                                    start=first_z[0], stop=False)
                    first_z[0] = False
                    last = i
                return em.ev(last, "T")

            pcount = [0]
            started = [False] * 4
            first_z[0] = True
            em.wait(em.t, em.bfree[5])
            for kc in (range(3) if 0 in DBG_BR else ()):
                for qh in range(4):
                    em.wait(em.t, em.bfree[6], *(lds if kc == 0 else ()),
                            (ev_arena[4 * h + qh] if kc == 0 else None),
                            (prev_slot_fence if not started[qh] else None),
                            (em.bfree[qh] if not started[qh] else None))
                    i = em.t.matmul(SIM[...], ckf[:, kc * 128:(kc + 1) * 128],
                                    arena[4 * h + qh][:, s * TS:(s + 1) * TS],
                                    start=True, stop=True)
                    last_pe[0] = em.ev(i, "T")
                    rot, ep = pexp_mask(TS, mskc[:, (s * 3 + kc) * TS:(s * 3 + kc + 1) * TS],
                                        SCALE)
                    em.wait(em.t, ep)
                    last_pe[0] = pv_z(rot, cvt2[:, kc * 128:(kc + 1) * 128], qh,
                                      not started[qh])
                    started[qh] = True
            ev = None
            if 0 in DBG_BR:
                ev = em.dfence(evac(pcount[0], 0, last_pe[0]))
                pcount[0] += 1
            started = [False] * 4
            first_z[0] = True
            for kc in (range(KC_SLOT[s]) if (1 in DBG_BR and 3 not in DBG_BR) else ()):
                tileT = selTa if kc < 16 else selTb
                vv = kc % 16
                em.wait(em.t, ev_selT, maskf, em.bfree[4])
                i = em.t.matmul(banks[4][...], eex[:, vv * 128:(vv + 1) * 128],
                                tileT[...], start=True, stop=True)
                eE = em.ev(i, "T")
                em.wait(em.v, eE, ("G", em.sc["G"]))
                iv = em.v.tensor_copy(sel01[kc % 2][...], banks[4][...])
                eS = em.ev(iv, "V")
                em.bfree[4] = eS
                for qh in range(4):
                    em.wait(em.t, em.bfree[6], ev_rq[qh],
                            (em.bfree[qh] if not started[qh] else None),
                            (ev if not started[qh] else None))
                    i = em.t.matmul(SIM[...], rkfm[:, kc * 128:(kc + 1) * 128],
                                    RQ[qh][:, s * TS:(s + 1) * TS],
                                    start=True, stop=True)
                    last_pe[0] = em.ev(i, "T")
                    n = unit[0]
                    rot = n % 2
                    em.wait(em.a, last_pe[0], C["m48"][1], eP.get(n - 2))
                    ia = em.a.activation(pt[rot][...], SIM.ap()[...], AF.Exp,
                                         bias=C["m48"][0][...], scale=SCALE)
                    eA[n] = em.ev(ia, "A")
                    em.bfree[6] = eA[n]
                    em.wait(em.g, eA[n], eS, eP.get(n - 2))
                    ig = em.g.tensor_tensor(ptm[rot][...], pt[rot][...], sel01[kc % 2][...],
                                            ALU.mult)
                    eP[n] = em.ev(ig, "G")
                    unit[0] += 1
                    em.wait(em.t, eP[n])
                    last_pe[0] = pv_z(rot, vtm[:, kc * 128:(kc + 1) * 128], qh,
                                      not started[qh])
                    started[qh] = True
            for qh in (range(4) if (1 in DBG_BR or 3 in DBG_BR) else ()):
                for j in range(4):
                    em.wait(em.t, em.bfree[6], ev_rq[qh], *esl)
                    i = em.t.matmul(SIM.ap()[:, 0:128],
                                    slK[:, 128 + j * 128:256 + j * 128],
                                    RQ[qh][:, s * TS + j * 128:s * TS + (j + 1) * 128],
                                    start=True, stop=True)
                    last_pe[0] = em.ev(i, "T")
                    n = unit[0]
                    rot = n % 2
                    em.wait(em.a, last_pe[0], C["m48"][1], eP.get(n - 2))
                    ia = em.a.activation(pt[rot][:, 0:128], SIM.ap()[:, 0:128], AF.Exp,
                                         bias=C["m48"][0][...], scale=SCALE)
                    eA[n] = em.ev(ia, "A")
                    em.bfree[6] = eA[n]
                    em.wait(em.g, eA[n], C["diagm"][1], eP.get(n - 2))
                    ig = em.g.tensor_tensor(ptm[rot][:, 0:128], pt[rot][:, 0:128],
                                            C["diagm"][0][...], ALU.mult)
                    eP[n] = em.ev(ig, "G")
                    unit[0] += 1
                    em.wait(em.t, eP[n])
                    if first_z[0]:
                        em.wait(em.t, em.bfree[5])
                    if not started[qh]:
                        em.wait(em.t, em.bfree[qh], ev)
                    em.t.matmul(banks[qh][:, j * 128:(j + 1) * 128], ptm[rot][:, 0:128],
                                slV[:, (1 + j) * 128:(2 + j) * 128],
                                start=(not started[qh] and j == 0), stop=False)
                    i = em.t.matmul(ZB.ap()[:, qh * 4 + j:qh * 4 + j + 1], ptm[rot][:, 0:128],
                                    C["onescol_b"][0][...], start=first_z[0], stop=False)
                    first_z[0] = False
                    started[qh] = True
                    last_pe[0] = em.ev(i, "T")
            if 1 in DBG_BR or 3 in DBG_BR:
                ev = em.dfence(evac(pcount[0], 1, last_pe[0]))
                pcount[0] += 1
            started = [False] * 4
            first_z[0] = True
            for kc in (range(5) if 2 in DBG_BR else ()):
                for qh in range(4):
                    em.wait(em.t, em.bfree[6], *(esl if kc == 0 and qh == 0 else ()),
                            ev_rq[qh], (em.bfree[qh] if not started[qh] else None),
                            (ev if not started[qh] else None))
                    i = em.t.matmul(SIM[...], slK[:, kc * 128:(kc + 1) * 128],
                                    RQ[qh][:, s * TS:(s + 1) * TS],
                                    start=True, stop=True)
                    last_pe[0] = em.ev(i, "T")
                    rot, ep = pexp_mask(TS, mskl[:, (s * 5 + kc) * TS:(s * 5 + kc + 1) * TS],
                                        SCALE)
                    em.wait(em.t, ep)
                    last_pe[0] = pv_z(rot, slV[:, kc * 128:(kc + 1) * 128], qh,
                                      not started[qh])
                    started[qh] = True
            if 2 in DBG_BR:
                ev = em.dfence(evac(pcount[0], 2, last_pe[0]))
                pcount[0] += 1
            prev_slot_fence = ev
            for qh in range(4):
                for j in range(4):
                    em.wait(em.v, ev)
                    iv = em.v.tensor_copy(otmp[...], otj[qh][:, j * 128:(j + 1) * 128])
                    ec = em.ev(iv, "V")
                    ev_attnout[4 * h + qh] = transpose128(
                        otmp[...], arena[4 * h + qh][:, (s * 4 + j) * 128:(s * 4 + j + 1) * 128],
                        (ec,))
        if h + 1 < KVH:
            lds = load_kvh(h + 1, last_pe[0])
    if STOP_AFTER in ("RQD", "SLKD"):
        pb.pop()
        pf.pop()
        pb.pop()
        return em.fence("DG") + em.fence("DY")
    if STOP_AFTER == "ATT":
        for c in range(NF):
            em.dma(em.g, "DG", dr["yout"][c * 128:(c + 1) * 128, :], arena[c][...],
                   waits=(ev_attnout[c],))
        pb.pop()
        pf.pop()
        pb.pop()
        return em.fence("DG") + em.fence("DY")
    ev_wo = [None] * 32
    y0b = pf.al(128, TS)
    xob = [pf.al(128, TS) for _ in range(2)]
    ev_vo = [None] * NF
    e_st2 = [None]
    for ob in range(NF):
        w = [ev_vo[ob - 2]] if ob >= 2 else [ev_qkvT[31]]
        ev_wo[ob] = em.dma(em.g, "DG", wqb[ob % 2][...], dr["woT"][ob],
                           waits=w, max_dma_last_dim=2048)
        for half in range(2):
            bk = banks[2 + ob % 2]
            for c in range(NF):
                w2_ = [ev_attnout[c]]
                if c == 0:
                    w2_ += [ev_wo[ob], em.bfree[2 + ob % 2]]
                em.wait(em.t, *w2_)
                i = em.t.matmul(bk[...], wqb[ob % 2][:, c * 128:(c + 1) * 128],
                                arena[c][:, half * TS:(half + 1) * TS],
                                start=(c == 0), stop=(c == NF - 1))
            ei = em.ev(i, "T")
            e_xo = em.dma(em.g, "DG", xob[half][...],
                          dr["xres1"][ob * 128:(ob + 1) * 128, half * TS:(half + 1) * TS],
                          waits=([ev_vo[ob - 1]] if ob >= 1 else ()))
            em.wait(em.v, ei, e_xo, e_st2[0])
            iv = em.v.tensor_tensor(y0b[...], bk[...], xob[half][...], ALU.add)
            evv = em.ev(iv, "V")
            em.bfree[2 + ob % 2] = evv
            ev_vo[ob] = evv
            e_st2[0] = em.dma(em.g, "DG",
                              dr["xres2"][ob * 128:(ob + 1) * 128, half * TS:(half + 1) * TS],
                              y0b[...], waits=(evv,))
    pb.pop()
    pf.pop()
    pb.pop()
    return em.fence("DG") + em.fence("DY")


def rope2(em, C, banks, RQ, arena, h, qh, ready_ev, scratch_ev, rcs2, rsn2, sws2, e_rs2):
    last = None
    for half in range(2):
        sl = slice(half * TS, (half + 1) * TS)
        em.wait(em.t, ready_ev, C["swapm"][1], em.bfree[6],
                scratch_ev if half == 0 else last)
        i = em.t.matmul(banks[6][...], C["swapm"][0][...], arena[4 * h + qh][:, sl],
                        start=True, stop=True)
        eswap = em.ev(i, "T")
        em.wait(em.v, eswap, e_rs2)
        i1 = em.v.tensor_tensor(sws2[...], banks[6][...], rsn2[:, sl], ALU.mult)
        e1 = em.ev(i1, "V")
        em.bfree[6] = e1
        em.wait(em.v, e1)
        i2 = em.v.tensor_tensor(RQ[qh][:, sl], arena[4 * h + qh][:, sl], rcs2[:, sl], ALU.mult)
        e2 = em.ev(i2, "V")
        em.wait(em.v, e2)
        i3 = em.v.tensor_tensor(RQ[qh][:, sl], RQ[qh][:, sl], sws2[...], ALU.add)
        last = em.ev(i3, "V")
    return last



def kernel(**inputs):
    sh = stage_shared(inputs)
    in_maps = []
    for r in range(8):
        m = dict(sh)
        m.update(stage_percore(inputs, r))
        in_maps.append(m)
    nc = build_program()
    res = run_bass_kernel_spmd(nc, in_maps, core_ids=list(range(8)))
    x = np.asarray(inputs["L0_attn_hidden_states"])
    out = np.empty_like(x)
    for r in range(8):
        b, cA, cB = core_chunks(r)
        y = res.results[r]["yout"]
        out[b, cA * 512:cA * 512 + 512, :] = y[:, 0:512].T
        out[b, cB * 512:cB * 512 + 512, :] = y[:, 512:1024].T
    return out
